# version 12
# speedup vs baseline: 1.1618x; 1.0024x over previous
import numpy as np
import ml_dtypes
from contextlib import ExitStack
import concourse.bass as bass
import concourse.mybir as mybir
from concourse.bass_utils import run_bass_kernel_spmd

F32 = mybir.dt.float32
BF16 = mybir.dt.bfloat16
FP16 = mybir.dt.float16
I32 = mybir.dt.int32
AF = mybir.ActivationFunctionType
ALU = mybir.AluOpType
AX = mybir.AxisListType

D = 4096
T = 2048
NCH = D // 128
TCH = T // 512
NEG = -30000.0
BIG = 16384.0
SCALE = 1.0 / np.sqrt(128.0)
KREM = 410
BF = ml_dtypes.bfloat16


def np_cos_sin():
    inv_freq = 10000.0 ** (-np.arange(64, dtype=np.float64) / 64.0)
    ang = np.arange(T, dtype=np.float64)[None, :] * inv_freq[:, None]
    cos = np.cos(ang).astype(np.float32)
    sin = np.sin(ang).astype(np.float32)
    return np.concatenate([cos, cos], 0), np.concatenate([sin, sin], 0)


def np_rot_lhsT():
    R0 = np.zeros((128, 128), dtype=np.float32)
    for f in range(64):
        R0[f, f + 64] = -1.0
        R0[f + 64, f] = 1.0
    return np.ascontiguousarray(R0.T).astype(BF)


def np_diag_masks():
    m = np.zeros((4, 128, 512), dtype=np.float32)
    for tt in range(4):
        for i in range(128):
            m[tt, i, 128 * tt + i + 1:] = NEG
    return m


class Proxy:

    def __init__(self, raw, p):
        self._raw = raw
        self._p = p

    def wait_ge(self, sem, val):
        return self._raw.wait_ge(sem, val + self._p.base.get(sem.name, 0))

    def __getattr__(self, k):
        return getattr(self._raw, k)


RACE_DETECT = True


class P:
    def __init__(self):
        self.nc = bass.Bass("TRN2", target_bir_lowering=False,
                            detect_race_conditions=RACE_DETECT)
        self.es = ExitStack()
        self.cnt = {}
        self.sems = {}
        self.base = {}
        self._phase = 0

    def new_phase(self):
        for k in list(self.cnt):
            self.base[k] = self.base.get(k, 0) + self.cnt[k]
            self.cnt[k] = 0
        self._phase += 1
        return f"ph{self._phase}_"

    def wrap(self, eng):
        return Proxy(eng, self)

    def sem(self, name):
        if name not in self.sems:
            self.sems[name] = self.es.enter_context(self.nc.semaphore(name))
            self.cnt[name] = 0
        return self.sems[name]

    def inc(self, instr, name, n=1):
        instr.then_inc(self.sem(name), n)
        self.cnt[name] += n
        return self.cnt[name]

    def sb(self, name, shape, dt=F32):
        return self.es.enter_context(self.nc.sbuf_tensor(name, shape, dt))

    def ps(self, name, shape, dt=F32):
        return self.es.enter_context(self.nc.psum_tensor(name, shape, dt))

    def inp(self, name, shape, dt=F32):
        return self.nc.dram_tensor(name, shape, dt, kind="ExternalInput").ap()

    def outp(self, name, shape, dt=F32):
        return self.nc.dram_tensor(name, shape, dt, kind="ExternalOutput").ap()

    def const(self, name, arr):
        return self.nc.inline_tensor(np.ascontiguousarray(arr), name).ap()

    def stream(self, name, quantum=16):
        return Stream(self, name, quantum)


class Stream:

    def __init__(self, p, name, quantum):
        self.p, self.name, self.q = p, name, quantum

    def inc(self, instr, idx, n=None):
        self.p.inc(instr, f"{self.name}{idx % 2}", n if n is not None else self.q)

    def wait(self, eng, idx):
        eng.wait_ge(self.p.sem(f"{self.name}{idx % 2}"), self.q * (idx // 2 + 1))

    def wait_all(self, eng, total_idx):
        for par in range(2):
            n = (total_idx - par + 1) // 2
            if n > 0:
                eng.wait_ge(self.p.sem(f"{self.name}{par}"), self.q * n)


def build_dequant(p=None, io=None):
    if p is None:
        p = P()
        io = {"qw": p.inp("qw", [512, 2048], I32), "sc": p.inp("sc", [512, 32]),
              "zr": p.inp("zr", [512, 32]), "wT": p.outp("wT", [4, 4096, 128], BF16)}
    nc = p.nc
    pref = p.new_phase()
    ctx = ExitStack()
    sb = lambda name, shape, dt=F32: ctx.enter_context(nc.sbuf_tensor(pref + name, shape, dt))
    ps = lambda name, shape, dt=F32: ctx.enter_context(nc.psum_tensor(pref + name, shape, dt))
    const = lambda name, arr: nc.inline_tensor(np.ascontiguousarray(arr), pref + name).ap()
    ldq = p.stream("ldq", 48)
    stq = p.stream("stq", 64)
    qw, sc, zr, wT = io["qw"], io["sc"], io["zr"], io["wT"]
    identc = const("identc", np.eye(128, dtype=BF))

    qt_ = sb("qt", [128, 2, 2048], I32); qt = [qt_[:, 0], qt_[:, 1]]
    sct_ = sb("sct", [128, 2, 32]); sct = [sct_[:, 0], sct_[:, 1]]
    zrt_ = sb("zrt", [128, 2, 32]); zrt = [zrt_[:, 0], zrt_[:, 1]]
    loi = sb("loi", [128, 256], I32)
    hii = sb("hii", [128, 256], I32)
    lof = sb("lof", [128, 256])
    hif = sb("hif", [128, 256])
    wch_ = sb("wch", [128, 2, 512], BF16); wch = [wch_[:, 0], wch_[:, 1]]
    ident = sb("ident", [128, 128], BF16)
    wst_ = sb("wst", [128, 2, 512], BF16); wst = [wst_[:, 0], wst_[:, 1]]
    pst = ps("pst", [128, 512], BF16)

    with ctx, nc.Block() as blk:
        @blk.sync
        def _(sync):
            sync = p.wrap(sync)
            p.inc(sync.dma_start(out=ident[:], in_=identc), "ld", 16)
            for j in range(4):
                if j >= 2:
                    sync.wait_ge(p.sem("sv"), (j - 1) * 8)
                i = sync.dma_start(out=qt[j % 2], in_=qw[128 * j:128 * (j + 1), :])
                ldq.inc(i, j, 16)
                i = sync.dma_start(out=sct[j % 2], in_=sc[128 * j:128 * (j + 1), :])
                ldq.inc(i, j, 16)
                i = sync.dma_start(out=zrt[j % 2], in_=zr[128 * j:128 * (j + 1), :])
                ldq.inc(i, j, 16)

        @blk.vector
        def _(vector):
            vector = p.wrap(vector)
            for j in range(4):
                ldq.wait(vector, j)
                for qc in range(8):
                    k = j * 8 + qc
                    if k >= 2:
                        vector.wait_ge(p.sem("st_pe"), k - 1)
                    vector.tensor_scalar(out=loi[:], in0=qt[j % 2][:, 256 * qc:256 * (qc + 1)],
                                         scalar1=15, scalar2=None, op0=ALU.bitwise_and)
                    i1 = vector.tensor_scalar(out=hii[:], in0=qt[j % 2][:, 256 * qc:256 * (qc + 1)],
                                              scalar1=4, scalar2=15,
                                              op0=ALU.logical_shift_right, op1=ALU.bitwise_and)
                    p.inc(i1, "vr")
                    vector.wait_ge(p.sem("vr"), p.cnt["vr"])
                    vector.tensor_copy(out=lof[:], in_=loi[:])
                    i2 = vector.tensor_copy(out=hif[:], in_=hii[:])
                    p.inc(i2, "vr")
                    vector.wait_ge(p.sem("vr"), p.cnt["vr"])
                    wv = wch[k % 2].rearrange("p (n two) -> p n two", two=2)
                    for g in range(4):
                        gg = 4 * qc + g
                        vector.tensor_scalar(
                            out=wv[:, 64 * g:64 * (g + 1), 0], in0=lof[:, 64 * g:64 * (g + 1)],
                            scalar1=zrt[j % 2][:, gg:gg + 1], scalar2=sct[j % 2][:, gg:gg + 1],
                            op0=ALU.subtract, op1=ALU.mult)
                        i3 = vector.tensor_scalar(
                            out=wv[:, 64 * g:64 * (g + 1), 1], in0=hif[:, 64 * g:64 * (g + 1)],
                            scalar1=zrt[j % 2][:, gg:gg + 1], scalar2=sct[j % 2][:, gg:gg + 1],
                            op0=ALU.subtract, op1=ALU.mult)
                    p.inc(i3, "sv")

        @blk.tensor
        def _(tensor):
            tensor = p.wrap(tensor)
            tensor.wait_ge(p.sem("ld"), 16)
            for j in range(4):
                for qc in range(8):
                    k = j * 8 + qc
                    tensor.wait_ge(p.sem("sv"), k + 1)
                    if k >= 1:
                        tensor.wait_ge(p.sem("ss"), k)
                    for b in range(4):
                        i4 = tensor.transpose(pst[:, 128 * b:128 * (b + 1)],
                                              wch[k % 2][:, 128 * b:128 * (b + 1)], ident[:])
                    p.inc(i4, "st_pe")

        @blk.scalar
        def _(scalar):
            scalar = p.wrap(scalar)
            for j in range(4):
                for qc in range(8):
                    k = j * 8 + qc
                    scalar.wait_ge(p.sem("st_pe"), k + 1)
                    if k >= 2:
                        stq.wait(scalar, k - 2)
                    i5 = scalar.copy(out=wst[k % 2], in_=pst[:])
                    p.inc(i5, "ss")
                    scalar.wait_ge(p.sem("ss"), p.cnt["ss"])
                    for b in range(4):
                        i6 = scalar.dma_start(out=wT[j, 512 * qc + 128 * b:512 * qc + 128 * (b + 1), :],
                                              in_=wst[k % 2][:, 128 * b:128 * (b + 1)])
                        stq.inc(i6, k, 16)
            stq.wait_all(scalar, 32)

    return nc


def build_ffn0(rscale=0.5, p=None, io=None):
    if p is None:
        p = P()
        io = {"x": p.inp("x", [D, T]), "w4": p.inp("w4", [NCH, D, 128], BF16),
              "xo": p.outp("xo", [D, T])}
    nc = p.nc
    pref = p.new_phase()
    ctx = ExitStack()
    sb = lambda name, shape, dt=F32: ctx.enter_context(nc.sbuf_tensor(pref + name, shape, dt))
    ps = lambda name, shape, dt=F32: ctx.enter_context(nc.psum_tensor(pref + name, shape, dt))
    const = lambda name, arr: nc.inline_tensor(np.ascontiguousarray(arr), pref + name).ap()
    ldw = p.stream("ldw")
    ldx = p.stream("ldx")
    sto = p.stream("sto")
    x, w4, xo = io["x"], io["w4"], io["xo"]
    onesP_c = const("onesP_c", np.ones((128, 1), dtype=BF))
    onesrf_c = const("onesrf_c", np.ones((1, 128), dtype=np.float32))

    xb = sb("xb", [128, NCH * T], BF16)

    def xbv(ic):
        return xb[:].rearrange("p (c t) -> p c t", c=NCH)[:, ic, :]

    sq_ = sb("sq", [128, 2, 512], BF16); sq = [sq_[:, 0], sq_[:, 1]]
    onesP = sb("onesP", [128, 1], BF16)
    onesrf = sb("onesrf", [1, 128])
    m_sb = sb("m_sb", [1, T])
    sd_sb = sb("sd_sb", [1, T])
    r_sb = sb("r_sb", [1, T])
    mr_sb = sb("mr_sb", [1, T])
    rb = sb("rb", [128, T])
    mrb = sb("mrb", [128, T])
    wbuf_ = sb("wbuf", [128, 2, NCH * 128], BF16); wbuf = [wbuf_[:, 0], wbuf_[:, 1]]

    def wbv(b, ic):
        return wbuf[b].rearrange("p (c o) -> p c o", c=NCH)[:, ic, :]

    csn_ = sb("csn", [128, 2, 1]); csn = [csn_[:, 0], csn_[:, 1]]
    xres_ = sb("xres", [128, 2, 512]); xres = [xres_[:, 0], xres_[:, 1]]
    u_ = sb("u", [128, 2, 512]); u = [u_[:, 0], u_[:, 1]]
    outsb_ = sb("outsb", [128, 2, 512]); outsb = [outsb_[:, 0], outsb_[:, 1]]

    ps_m1 = ps("ps_m1", [1, 512])
    ps_q1 = ps("ps_q1", [1, 512])
    ps_b = ps("ps_b", [128, 512])
    ps_cs = ps("ps_cs", [128, 1])
    ps_z_ = ps("ps_z", [128, 2, 512]); ps_z = [ps_z_[:, 0], ps_z_[:, 1]]

    with ctx, nc.Block() as blk:
        @blk.gpsimd
        def _(gp):
            gp = p.wrap(gp)
            for ic in range(NCH):
                i = gp.dma_start(out=xbv(ic), in_=x[128 * ic:128 * (ic + 1), :])
                p.inc(i, "gx", 16)
            for oc in range(NCH):
                for tc in range(TCH):
                    g = TCH * oc + tc
                    gp.wait_ge(p.sem("vu"), g + 1)
                    gp.wait_ge(p.sem("vt1"), g + 1)
                    i = gp.tensor_add(out=outsb[g % 2], in0=outsb[g % 2], in1=u[g % 2])
                    p.inc(i, "gout")

        @blk.sync
        def _(sync):
            sync = p.wrap(sync)
            p.inc(sync.dma_start(out=onesP[:], in_=onesP_c), "ld1", 16)
            p.inc(sync.dma_start(out=onesrf[:], in_=onesrf_c), "ld1", 16)
            for oc in range(NCH):
                if oc >= 2:
                    sync.wait_ge(p.sem("pez"), TCH * (oc - 1))
                i = sync.dma_start(out=wbuf[oc % 2].rearrange("p (c o) -> p c o", c=NCH),
                                   in_=w4[oc].rearrange("(c p) o -> p c o", p=128))
                ldw.inc(i, oc)
                for tc in range(TCH):
                    g = TCH * oc + tc
                    if g >= 2:
                        sync.wait_ge(p.sem("vu"), g - 1)
                    i = sync.dma_start(out=xres[g % 2],
                                       in_=x[128 * oc:128 * (oc + 1), 512 * tc:512 * (tc + 1)])
                    ldx.inc(i, g)

        @blk.scalar
        def _(scalar):
            scalar = p.wrap(scalar)
            for tc in range(TCH):
                for ic in range(NCH):
                    k = NCH * tc + ic
                    if tc == 0 and ic == 0:
                        scalar.wait_ge(p.sem("gx"), 16 * NCH)
                    if k >= 2:
                        scalar.wait_ge(p.sem("pesq"), k - 1)
                    i = scalar.activation(out=sq[k % 2], in_=xbv(ic)[:, 512 * tc:512 * (tc + 1)],
                                          func=AF.Square)
                    p.inc(i, "ssq")
            scalar.wait_ge(p.sem("vvar"), 1)
            i = scalar.activation(out=sd_sb[:], in_=sd_sb[:], func=AF.Sqrt)
            p.inc(i, "sact")
            for k in range(2):
                for tc in range(TCH):
                    n = TCH * k + tc
                    scalar.wait_ge(p.sem("pebc"), n + 1)
                    dst = rb if k == 0 else mrb
                    i = scalar.copy(out=dst[:, 512 * tc:512 * (tc + 1)], in_=ps_b[:])
                    p.inc(i, "sbc")
            def store_oc(oc):
                for tc in range(TCH):
                    g = TCH * oc + tc
                    scalar.wait_ge(p.sem("gout"), g + 1)
                    i = scalar.dma_start(out=xo[128 * oc:128 * (oc + 1), 512 * tc:512 * (tc + 1)],
                                         in_=outsb[g % 2])
                    sto.inc(i, g)
            for oc in range(NCH):
                scalar.wait_ge(p.sem("pecs"), oc + 1)
                i = scalar.activation(out=csn[oc % 2], in_=ps_cs[:], func=AF.Copy, scale=-1.0)
                p.inc(i, "scs")
                if oc >= 1:
                    store_oc(oc - 1)
            store_oc(NCH - 1)
            sto.wait_all(scalar, NCH * TCH)

        @blk.tensor
        def _(tensor):
            tensor = p.wrap(tensor)
            tensor.wait_ge(p.sem("ld1"), 32)
            for tc in range(TCH):
                if tc >= 1:
                    tensor.wait_ge(p.sem("vstat"), tc)
                for ic in range(NCH):
                    k = NCH * tc + ic
                    if tc == 0 and ic == 0:
                        tensor.wait_ge(p.sem("gx"), 16 * NCH)
                    tensor.wait_ge(p.sem("ssq"), k + 1)
                    tensor.matmul(ps_m1[:], onesP[:], xbv(ic)[:, 512 * tc:512 * (tc + 1)],
                                  start=(ic == 0), stop=(ic == NCH - 1))
                    i = tensor.matmul(ps_q1[:], onesP[:], sq[k % 2],
                                      start=(ic == 0), stop=(ic == NCH - 1))
                    p.inc(i, "pesq")
            tensor.wait_ge(p.sem("vmr"), 1)
            for k in range(2):
                src = r_sb if k == 0 else mr_sb
                for tc in range(TCH):
                    n = TCH * k + tc
                    if n >= 1:
                        tensor.wait_ge(p.sem("sbc"), n)
                    i = tensor.matmul(ps_b[:], onesrf[:], src[:, 512 * tc:512 * (tc + 1)],
                                      start=True, stop=True)
                    p.inc(i, "pebc")
            for oc in range(NCH):
                ldw.wait(tensor, oc)
                tensor.wait_ge(p.sem("scs"), oc)
                for ic in range(NCH):
                    i = tensor.matmul(ps_cs[:], wbv(oc % 2, ic), onesP[:],
                                      start=(ic == 0), stop=(ic == NCH - 1))
                p.inc(i, "pecs")
                for tc in range(TCH):
                    g = TCH * oc + tc
                    if g >= 2:
                        tensor.wait_ge(p.sem("vt1"), g - 1)
                    for ic in range(NCH):
                        i = tensor.matmul(ps_z[g % 2], wbv(oc % 2, ic),
                                          xbv(ic)[:, 512 * tc:512 * (tc + 1)],
                                          start=(ic == 0), stop=(ic == NCH - 1))
                    p.inc(i, "pez")

        @blk.vector
        def _(vector):
            vector = p.wrap(vector)
            def fence(i):
                p.inc(i, "vr")
                vector.wait_ge(p.sem("vr"), p.cnt["vr"])

            for tc in range(TCH):
                vector.wait_ge(p.sem("pesq"), NCH * (tc + 1))
                vector.tensor_scalar(out=m_sb[:, 512 * tc:512 * (tc + 1)], in0=ps_m1[:],
                                     scalar1=1.0 / D, scalar2=None, op0=ALU.mult)
                i = vector.tensor_scalar(out=sd_sb[:, 512 * tc:512 * (tc + 1)], in0=ps_q1[:],
                                         scalar1=1.0 / D, scalar2=1e-5, op0=ALU.mult, op1=ALU.add)
                p.inc(i, "vstat")
            vector.wait_ge(p.sem("vstat"), TCH)
            fence(vector.tensor_mul(out=r_sb[:], in0=m_sb[:], in1=m_sb[:]))
            i = vector.tensor_sub(out=sd_sb[:], in0=sd_sb[:], in1=r_sb[:])
            p.inc(i, "vvar")
            vector.wait_ge(p.sem("sact"), 1)
            fence(vector.reciprocal(out=r_sb[:], in_=sd_sb[:]))
            fence(vector.tensor_scalar(out=r_sb[:], in0=r_sb[:], scalar1=rscale,
                                       scalar2=None, op0=ALU.mult))
            i = vector.tensor_mul(out=mr_sb[:], in0=m_sb[:], in1=r_sb[:])
            p.inc(i, "vmr")
            vector.wait_ge(p.sem("sbc"), 8)
            for oc in range(NCH):
                vector.wait_ge(p.sem("scs"), oc + 1)
                for tc in range(TCH):
                    g = TCH * oc + tc
                    ldx.wait(vector, g)
                    if g >= 2:
                        vector.wait_ge(p.sem("gout"), g - 1)
                    i = vector.scalar_tensor_tensor(
                        out=u[g % 2], in0=mrb[:, 512 * tc:512 * (tc + 1)],
                        scalar=csn[oc % 2][:, 0:1], in1=xres[g % 2],
                        op0=ALU.mult, op1=ALU.add)
                    p.inc(i, "vu")
                    vector.wait_ge(p.sem("pez"), g + 1)
                    if g >= 2:
                        sto.wait(vector, g - 2)
                    i = vector.tensor_mul(out=outsb[g % 2], in0=ps_z[g % 2],
                                          in1=rb[:, 512 * tc:512 * (tc + 1)])
                    p.inc(i, "vt1")

    return nc


def prep_ffn0_inputs(x_fm, wT_full):
    return {"x": np.ascontiguousarray(x_fm), "w4": np.ascontiguousarray(wT_full)}


def run_phase(nc, in_maps, n_cores=None):
    cores = list(range(len(in_maps)))
    return run_bass_kernel_spmd(nc, in_maps, core_ids=cores).results


def sim_phase(nc, in_map):
    from concourse.bass_interp import CoreSim
    sim = CoreSim(nc)
    for k, v in in_map.items():
        sim.tensor(k)[:] = v
    sim.simulate(check_with_hw=False)
    return sim


def build_ffn1(rscale=0.5, p=None, io=None):
    if p is None:
        p = P()
        io = {"x": p.inp("x", [D, T]), "V": p.inp("V", [D, 1024]),
              "UT": p.inp("UT", [1024, D]), "St": p.inp("St", [128, 8]),
              "xo": p.outp("xo", [D, T])}
    nc = p.nc
    pref = p.new_phase()
    ctx = ExitStack()
    sb = lambda name, shape, dt=F32: ctx.enter_context(nc.sbuf_tensor(pref + name, shape, dt))
    ps = lambda name, shape, dt=F32: ctx.enter_context(nc.psum_tensor(pref + name, shape, dt))
    const = lambda name, arr: nc.inline_tensor(np.ascontiguousarray(arr), pref + name).ap()
    gx = p.stream("gx")
    gv = p.stream("gv")
    gu2 = p.stream("gu2")
    ldx = p.stream("ldx")
    sto = p.stream("sto")
    x, V, UT, St, xo = io["x"], io["V"], io["UT"], io["St"], io["xo"]
    onesP_c = const("onesP_c", np.ones((128, 1), dtype=BF))
    onesrf_c = const("onesrf_c", np.ones((1, 128), dtype=np.float32))

    xbs_ = sb("xbs", [128, 2, T], BF16); xbs = [xbs_[:, 0], xbs_[:, 1]]
    sqs_ = sb("sqs", [128, 2, T], BF16); sqs = [sqs_[:, 0], sqs_[:, 1]]
    onesP = sb("onesP", [128, 1], BF16)
    onesrf = sb("onesrf", [1, 128])
    m_sb = sb("m_sb", [1, T])
    sd_sb = sb("sd_sb", [1, T])
    r_sb = sb("r_sb", [1, T])
    mr_sb = sb("mr_sb", [1, T])
    rb = sb("rb", [128, T])
    mrb = sb("mrb", [128, T])
    Vb_ = sb("Vb", [128, 2, NCH * 256], BF16); Vb = [Vb_[:, 0], Vb_[:, 1]]
    UTb_ = sb("UTb", [128, 2, 8 * 128], BF16); UTb = [UTb_[:, 0], UTb_[:, 1]]
    Stt = sb("Stt", [128, 8])
    csnV = sb("csnV", [128, 2])
    tmp = sb("tmp", [128, 8 * 512])
    zb = sb("zb", [128, 8 * T], BF16)
    xres_ = sb("xres", [128, 2, 512]); xres = [xres_[:, 0], xres_[:, 1]]
    outsb_ = sb("outsb", [128, 2, 512]); outsb = [outsb_[:, 0], outsb_[:, 1]]
    ps_all = ps("ps_all", [128, 8 * 512])

    def bank(b, rows=128):
        return ps_all[0:rows, 512 * b:512 * (b + 1)]

    def Vbv(b, ic, sub):
        return Vb[b].rearrange("p (c r) -> p c r", c=NCH)[:, ic, 128 * sub:128 * (sub + 1)]

    def UTv(b, rc):
        return UTb[b].rearrange("p (c o) -> p c o", c=8)[:, rc, :]

    def zbv(rc):
        return zb[:].rearrange("p (c t) -> p c t", c=8)[:, rc, :]

    ps_m = ps_all[0:1, 0:T]
    ps_q = ps_all[0:1, T:2 * T]

    with ctx, nc.Block() as blk:
        @blk.gpsimd
        def _(gp):
            gp = p.wrap(gp)
            for ps in range(5):
                if ps >= 1:
                    rp = ps - 1
                    if rp >= 2:
                        gp.wait_ge(p.sem("pexc"), 32 * rp)
                    i = gp.dma_start(out=Vb[rp % 2].rearrange("p (c r) -> p c r", c=NCH),
                                     in_=V[:, 256 * rp:256 * (rp + 1)].rearrange("(c p) r -> p c r", p=128))
                    gv.inc(i, rp)
                for ic in range(NCH):
                    idx = 32 * ps + ic
                    if idx >= 2:
                        gp.wait_ge(p.sem("pexc"), idx - 1)
                    i = gp.dma_start(out=xbs[idx % 2], in_=x[128 * ic:128 * (ic + 1), :])
                    gx.inc(i, idx)
            for oc in range(NCH):
                if oc >= 2:
                    gp.wait_ge(p.sem("pey"), TCH * (oc - 1))
                i = gp.dma_start(out=UTb[oc % 2].rearrange("p (c o) -> p c o", c=8),
                                 in_=UT[:, 128 * oc:128 * (oc + 1)].rearrange("(c p) o -> p c o", p=128))
                gu2.inc(i, oc)

        @blk.sync
        def _(sync):
            sync = p.wrap(sync)
            p.inc(sync.dma_start(out=onesP[:], in_=onesP_c), "ld1", 16)
            p.inc(sync.dma_start(out=onesrf[:], in_=onesrf_c), "ld1", 16)
            p.inc(sync.dma_start(out=Stt[:], in_=St), "ld1", 16)
            for oc in range(NCH):
                for tc in range(TCH):
                    g = TCH * oc + tc
                    if g >= 2:
                        sync.wait_ge(p.sem("vy"), g - 1)
                    i = sync.dma_start(out=xres[g % 2],
                                       in_=x[128 * oc:128 * (oc + 1), 512 * tc:512 * (tc + 1)])
                    ldx.inc(i, g)

        @blk.scalar
        def _(scalar):
            scalar = p.wrap(scalar)
            for ic in range(NCH):
                gx.wait(scalar, ic)
                if ic >= 2:
                    scalar.wait_ge(p.sem("pexc"), ic - 1)
                i = scalar.activation(out=sqs[ic % 2], in_=xbs[ic % 2], func=AF.Square)
                p.inc(i, "ssq")
            scalar.wait_ge(p.sem("vvar"), 1)
            i = scalar.activation(out=sd_sb[:], in_=sd_sb[:], func=AF.Sqrt)
            p.inc(i, "sact")
            for k in range(2):
                for tc in range(TCH):
                    n = TCH * k + tc
                    scalar.wait_ge(p.sem("pebc"), n + 1)
                    dst = rb if k == 0 else mrb
                    i = scalar.copy(out=dst[:, 512 * tc:512 * (tc + 1)], in_=bank(2))
                    p.inc(i, "sbc")
            for rp in range(4):
                scalar.wait_ge(p.sem("pecs"), rp + 1)
                if rp >= 1:
                    scalar.wait_ge(p.sem("vevac"), rp)
                i = scalar.activation(out=csnV[:], in_=ps_all[:, 512 * 7:512 * 7 + 2],
                                      func=AF.Copy, scale=-1.0)
                p.inc(i, "scs")
            for oc in range(NCH):
                for tc in range(TCH):
                    g = TCH * oc + tc
                    scalar.wait_ge(p.sem("vy"), g + 1)
                    i = scalar.dma_start(out=xo[128 * oc:128 * (oc + 1), 512 * tc:512 * (tc + 1)],
                                         in_=outsb[g % 2])
                    sto.inc(i, g)
            sto.wait_all(scalar, NCH * TCH)

        @blk.tensor
        def _(tensor):
            tensor = p.wrap(tensor)
            tensor.wait_ge(p.sem("ld1"), 48)
            for ic in range(NCH):
                gx.wait(tensor, ic)
                tensor.wait_ge(p.sem("ssq"), ic + 1)
                for tc in range(TCH):
                    tensor.matmul(ps_m[:, 512 * tc:512 * (tc + 1)], onesP[:],
                                  xbs[ic % 2][:, 512 * tc:512 * (tc + 1)],
                                  start=(ic == 0), stop=(ic == NCH - 1))
                    i = tensor.matmul(ps_q[:, 512 * tc:512 * (tc + 1)], onesP[:],
                                      sqs[ic % 2][:, 512 * tc:512 * (tc + 1)],
                                      start=(ic == 0), stop=(ic == NCH - 1))
                p.inc(i, "pexc")
            tensor.wait_ge(p.sem("vmr"), 1)
            for k in range(2):
                src = r_sb if k == 0 else mr_sb
                for tc in range(TCH):
                    n = TCH * k + tc
                    if n >= 1:
                        tensor.wait_ge(p.sem("sbc"), n)
                    i = tensor.matmul(bank(2), onesrf[:], src[:, 512 * tc:512 * (tc + 1)],
                                      start=True, stop=True)
                    p.inc(i, "pebc")
            tensor.wait_ge(p.sem("sbc"), 8)
            for rp in range(4):
                gv.wait(tensor, rp)
                if rp >= 1:
                    tensor.wait_ge(p.sem("vevac"), rp)
                for sub in range(2):
                    for ic in range(NCH):
                        i = tensor.matmul(ps_all[:, 512 * 7 + sub:512 * 7 + sub + 1],
                                          Vbv(rp % 2, ic, sub), onesP[:],
                                          start=(ic == 0), stop=(ic == NCH - 1))
                p.inc(i, "pecs")
                tensor.wait_ge(p.sem("scs"), rp + 1)
                for ic in range(NCH):
                    idx = 32 * (rp + 1) + ic
                    gx.wait(tensor, idx)
                    for sub in range(2):
                        for tc in range(TCH):
                            i = tensor.matmul(bank(4 * sub + tc), Vbv(rp % 2, ic, sub),
                                              xbs[idx % 2][:, 512 * tc:512 * (tc + 1)],
                                              start=(ic == 0), stop=(ic == NCH - 1))
                    p.inc(i, "pexc")
            for oc in range(NCH):
                gu2.wait(tensor, oc)
                for tc in range(TCH):
                    g = TCH * oc + tc
                    if g == 0:
                        tensor.wait_ge(p.sem("vevac"), 4)
                    if g >= 2:
                        tensor.wait_ge(p.sem("vy"), g - 1)
                    for rc in range(8):
                        i = tensor.matmul(bank(g % 2), UTv(oc % 2, rc),
                                          zbv(rc)[:, 512 * tc:512 * (tc + 1)],
                                          start=(rc == 0), stop=(rc == 7))
                    p.inc(i, "pey")

        @blk.vector
        def _(vector):
            vector = p.wrap(vector)
            def fence(i):
                p.inc(i, "vr")
                vector.wait_ge(p.sem("vr"), p.cnt["vr"])

            vector.wait_ge(p.sem("pexc"), NCH)
            for tc in range(TCH):
                vector.tensor_scalar(out=m_sb[:, 512 * tc:512 * (tc + 1)],
                                     in0=ps_m[:, 512 * tc:512 * (tc + 1)],
                                     scalar1=1.0 / D, scalar2=None, op0=ALU.mult)
                i = vector.tensor_scalar(out=sd_sb[:, 512 * tc:512 * (tc + 1)],
                                         in0=ps_q[:, 512 * tc:512 * (tc + 1)],
                                         scalar1=1.0 / D, scalar2=1e-5, op0=ALU.mult, op1=ALU.add)
                p.inc(i, "vstat")
            vector.wait_ge(p.sem("vstat"), TCH)
            fence(vector.tensor_mul(out=r_sb[:], in0=m_sb[:], in1=m_sb[:]))
            i = vector.tensor_sub(out=sd_sb[:], in0=sd_sb[:], in1=r_sb[:])
            p.inc(i, "vvar")
            vector.wait_ge(p.sem("sact"), 1)
            fence(vector.reciprocal(out=r_sb[:], in_=sd_sb[:]))
            fence(vector.tensor_scalar(out=r_sb[:], in0=r_sb[:], scalar1=rscale,
                                       scalar2=None, op0=ALU.mult))
            i = vector.tensor_mul(out=mr_sb[:], in0=m_sb[:], in1=r_sb[:])
            p.inc(i, "vmr")
            for rp in range(4):
                vector.wait_ge(p.sem("pexc"), 32 * (rp + 2))
                vector.wait_ge(p.sem("scs"), rp + 1)
                for sub in range(2):
                    for tc in range(TCH):
                        b = 4 * sub + tc
                        i = vector.tensor_mul(out=tmp[:, 512 * b:512 * (b + 1)], in0=bank(b),
                                              in1=rb[:, 512 * tc:512 * (tc + 1)])
                p.inc(i, "vr")
                vector.wait_ge(p.sem("vr"), p.cnt["vr"])
                for sub in range(2):
                    for tc in range(TCH):
                        b = 4 * sub + tc
                        i = vector.scalar_tensor_tensor(
                            out=tmp[:, 512 * b:512 * (b + 1)],
                            in0=mrb[:, 512 * tc:512 * (tc + 1)], scalar=csnV[:, sub:sub + 1],
                            in1=tmp[:, 512 * b:512 * (b + 1)], op0=ALU.mult, op1=ALU.add)
                        p.inc(i, "vr")
                vector.wait_ge(p.sem("vr"), p.cnt["vr"])
                for sub in range(2):
                    rc = 2 * rp + sub
                    for tc in range(TCH):
                        b = 4 * sub + tc
                        i = vector.tensor_scalar(out=zbv(rc)[:, 512 * tc:512 * (tc + 1)],
                                                 in0=tmp[:, 512 * b:512 * (b + 1)],
                                                 scalar1=Stt[:, rc:rc + 1], scalar2=None,
                                                 op0=ALU.mult)
                p.inc(i, "vevac")
            for oc in range(NCH):
                for tc in range(TCH):
                    g = TCH * oc + tc
                    vector.wait_ge(p.sem("pey"), g + 1)
                    ldx.wait(vector, g)
                    if g >= 2:
                        sto.wait(vector, g - 2)
                    i = vector.scalar_tensor_tensor(
                        out=outsb[g % 2], in0=bank(g % 2), scalar=1.0,
                        in1=xres[g % 2], op0=ALU.mult, op1=ALU.add)
                    p.inc(i, "vy")

    return nc


def build_attn_proj(p=None, io=None):
    if p is None:
        p = P()
        io = {"x": p.inp("x", [D, T]), "Wq": p.inp("Wq", [D, D]),
              "Wk": p.inp("Wk", [D, 1024]), "Wv": p.inp("Wv", [D, 1024]),
              "qr": p.outp("qr", [D, T], BF16), "kr": p.outp("kr", [1024, T], BF16),
              "vv": p.outp("vv", [1024, T], BF16)}
    nc = p.nc
    pref = p.new_phase()
    ctx = ExitStack()
    sb = lambda name, shape, dt=F32: ctx.enter_context(nc.sbuf_tensor(pref + name, shape, dt))
    ps = lambda name, shape, dt=F32: ctx.enter_context(nc.psum_tensor(pref + name, shape, dt))
    const = lambda name, arr: nc.inline_tensor(np.ascontiguousarray(arr), pref + name).ap()
    gx = p.stream("gx")
    gw = p.stream("gw")
    sto = p.stream("sto", 64)
    x, Wq, Wk, Wv = io["x"], io["Wq"], io["Wk"], io["Wv"]
    qr, kr, vv = io["qr"], io["kr"], io["vv"]
    onesP_c = const("onesP_c", np.ones((128, 1), dtype=BF))
    onesrf_c = const("onesrf_c", np.ones((1, 128), dtype=np.float32))
    cosT_np, sinT_np = np_cos_sin()
    cos_c = const("cos_c", cosT_np.astype(BF))
    sin_c = const("sin_c", sinT_np.astype(BF))
    rot_c = const("rot_c", np_rot_lhsT())

    xb = sb("xb", [128, NCH * T], BF16)

    def xbv(ic):
        return xb[:].rearrange("p (c t) -> p c t", c=NCH)[:, ic, :]

    sqs_ = sb("sqs", [128, 1, 512], BF16); sqs = [sqs_[:, 0], sqs_[:, 0]]
    onesP = sb("onesP", [128, 1], BF16)
    onesrf = sb("onesrf", [1, 128])
    cosT = sb("cosT", [128, T], BF16)
    sinT = sb("sinT", [128, T], BF16)
    rotT = sb("rotT", [128, 128], BF16)
    m_sb = sb("m_sb", [1, T])
    sd_sb = sb("sd_sb", [1, T])
    rb = sb("rb", [128, T])
    mrb = sb("mrb", [128, T])
    wbuf_ = sb("wbuf", [128, 1, NCH * 128], BF16); wbuf = [wbuf_[:, 0], wbuf_[:, 0]]

    def wbv(b, ic):
        return wbuf[b].rearrange("p (c o) -> p c o", c=NCH)[:, ic, :]

    csn_ = sb("csn", [128, 2, 1]); csn = [csn_[:, 0], csn_[:, 1]]
    t1s = sb("t1s", [128, T])
    yb = sb("yb", [128, T], BF16)
    yst_ = sb("yst", [128, 2, T], BF16); yst = [yst_[:, 0], yst_[:, 1]]
    ps_all = ps("ps_all", [128, 8 * 512])

    def bank(b, rows=128):
        return ps_all[0:rows, 512 * b:512 * (b + 1)]

    ps_m = ps_all[0:1, 0:T]
    ps_q = ps_all[0:1, T:2 * T]
    ps_cs = ps_all[:, 512 * 4:512 * 4 + 1]

    NU = 48

    def udst(u):
        if u < 32:
            return qr, 128 * u, True
        if u < 40:
            return kr, 128 * (u - 32), True
        return vv, 128 * (u - 40), False

    def usrc(u):
        if u < 32:
            return Wq, 128 * u
        if u < 40:
            return Wk, 128 * (u - 32)
        return Wv, 128 * (u - 40)

    nqk = lambda u: min(u, 40)

    with ctx, nc.Block() as blk:
        @blk.gpsimd
        def _(gp):
            gp = p.wrap(gp)
            for ic in range(NCH):
                i = gp.dma_start(out=xbv(ic), in_=x[128 * ic:128 * (ic + 1), :])
                gx.inc(i, ic)
            for u in range(NU):
                W, col = usrc(u)
                if u >= 1:
                    gp.wait_ge(p.sem("pez"), 4 * u)
                    gp.wait_ge(p.sem("pecs"), u)
                i = gp.dma_start(out=wbuf[u % 2].rearrange("p (c o) -> p c o", c=NCH),
                                 in_=W[:, col:col + 128].rearrange("(c p) o -> p c o", p=128))
                gw.inc(i, u)
                _, _, isqk = udst(u)
                if isqk:
                    gp.wait_ge(p.sem("vy"), 4 * u + 4)
                    if nqk(u) >= 1:
                        gp.wait_ge(p.sem("perot"), 4 * nqk(u))
                    i = gp.tensor_copy(out=yb[:], in_=t1s[:])
                    p.inc(i, "gyb")

        @blk.sync
        def _(sync):
            sync = p.wrap(sync)
            p.inc(sync.dma_start(out=onesP[:], in_=onesP_c), "ld1", 16)
            p.inc(sync.dma_start(out=onesrf[:], in_=onesrf_c), "ld1", 16)
            p.inc(sync.dma_start(out=cosT[:], in_=cos_c), "ld1", 16)
            p.inc(sync.dma_start(out=sinT[:], in_=sin_c), "ld1", 16)
            p.inc(sync.dma_start(out=rotT[:], in_=rot_c), "ld1", 16)

        @blk.scalar
        def _(scalar):
            scalar = p.wrap(scalar)
            for ic in range(NCH):
                for tc in range(TCH):
                    k = TCH * ic + tc
                    if k == 0:
                        gx.wait_all(scalar, NCH)
                    if k >= 1:
                        scalar.wait_ge(p.sem("pesq"), k)
                    i = scalar.activation(out=sqs[k % 2],
                                          in_=xbv(ic)[:, 512 * tc:512 * (tc + 1)], func=AF.Square)
                    p.inc(i, "ssq")
            scalar.wait_ge(p.sem("vvar"), 1)
            i = scalar.activation(out=sd_sb[:], in_=sd_sb[:], func=AF.Sqrt)
            p.inc(i, "sact")
            for k in range(2):
                for tc in range(TCH):
                    n = TCH * k + tc
                    scalar.wait_ge(p.sem("pebc"), n + 1)
                    dst = rb if k == 0 else mrb
                    i = scalar.copy(out=dst[:, 512 * tc:512 * (tc + 1)], in_=bank(2))
                    p.inc(i, "sbc")

            def store_u(u):
                dst, row, isqk = udst(u)
                if isqk:
                    scalar.wait_ge(p.sem("vyr"), 4 * (nqk(u) + 1))
                else:
                    scalar.wait_ge(p.sem("vy2"), 4 * (u - 39))
                for tc in range(TCH):
                    i = scalar.dma_start(out=dst[row:row + 128, 512 * tc:512 * (tc + 1)],
                                         in_=yst[u % 2][:, 512 * tc:512 * (tc + 1)])
                    sto.inc(i, u, 16)

            for u in range(NU):
                scalar.wait_ge(p.sem("pecs"), u + 1)
                i = scalar.activation(out=csn[u % 2], in_=ps_cs, func=AF.Copy, scale=-1.0)
                p.inc(i, "scs")
                if u >= 1:
                    store_u(u - 1)
            store_u(NU - 1)
            sto.wait_all(scalar, NU)

        @blk.tensor
        def _(tensor):
            tensor = p.wrap(tensor)
            tensor.wait_ge(p.sem("ld1"), 80)
            gx.wait_all(tensor, NCH)
            for ic in range(NCH):
                for tc in range(TCH):
                    k = TCH * ic + tc
                    tensor.wait_ge(p.sem("ssq"), k + 1)
                    tensor.matmul(ps_m[:, 512 * tc:512 * (tc + 1)], onesP[:],
                                  xbv(ic)[:, 512 * tc:512 * (tc + 1)],
                                  start=(ic == 0), stop=(ic == NCH - 1))
                    i = tensor.matmul(ps_q[:, 512 * tc:512 * (tc + 1)], onesP[:],
                                      sqs[k % 2], start=(ic == 0), stop=(ic == NCH - 1))
                    p.inc(i, "pesq")
            tensor.wait_ge(p.sem("vmr"), 1)
            for k in range(2):
                src = sd_sb if k == 0 else m_sb
                for tc in range(TCH):
                    n = TCH * k + tc
                    if n >= 1:
                        tensor.wait_ge(p.sem("sbc"), n)
                    i = tensor.matmul(bank(2), onesrf[:], src[:, 512 * tc:512 * (tc + 1)],
                                      start=True, stop=True)
                    p.inc(i, "pebc")
            tensor.wait_ge(p.sem("sbc"), 8)
            for u in range(NU):
                _, _, isqk = udst(u)
                gw.wait(tensor, u)
                tensor.wait_ge(p.sem("scs"), u)
                if nqk(u) >= 1:
                    tensor.wait_ge(p.sem("vu2"), 4 * nqk(u))
                for ic in range(NCH):
                    i = tensor.matmul(ps_cs, wbv(u % 2, ic), onesP[:],
                                      start=(ic == 0), stop=(ic == NCH - 1))
                p.inc(i, "pecs")
                for tc in range(TCH):
                    g = TCH * u + tc
                    if u >= 1:
                        tensor.wait_ge(p.sem("vt1"), 4 * (u - 1) + tc + 1)
                    for ic in range(NCH):
                        i = tensor.matmul(bank(tc), wbv(u % 2, ic),
                                          xbv(ic)[:, 512 * tc:512 * (tc + 1)],
                                          start=(ic == 0), stop=(ic == NCH - 1))
                    p.inc(i, "pez")
                if isqk:
                    tensor.wait_ge(p.sem("gyb"), nqk(u) + 1)
                    tensor.wait_ge(p.sem("scs"), u + 1)
                    for tc in range(TCH):
                        i = tensor.matmul(bank(4 + tc), rotT[:],
                                          yb[:, 512 * tc:512 * (tc + 1)], start=True, stop=True)
                        p.inc(i, "perot")

        @blk.vector
        def _(vector):
            vector = p.wrap(vector)
            def fence(i):
                p.inc(i, "vr")
                vector.wait_ge(p.sem("vr"), p.cnt["vr"])

            vector.wait_ge(p.sem("pesq"), NCH * TCH)
            for tc in range(TCH):
                vector.tensor_scalar(out=m_sb[:, 512 * tc:512 * (tc + 1)],
                                     in0=ps_m[:, 512 * tc:512 * (tc + 1)],
                                     scalar1=1.0 / D, scalar2=None, op0=ALU.mult)
                i = vector.tensor_scalar(out=sd_sb[:, 512 * tc:512 * (tc + 1)],
                                         in0=ps_q[:, 512 * tc:512 * (tc + 1)],
                                         scalar1=1.0 / D, scalar2=1e-5, op0=ALU.mult, op1=ALU.add)
                p.inc(i, "vstat")
            vector.wait_ge(p.sem("vstat"), TCH)
            fence(vector.tensor_mul(out=rb[0:1, 0:T], in0=m_sb[:], in1=m_sb[:]))
            i = vector.tensor_sub(out=sd_sb[:], in0=sd_sb[:], in1=rb[0:1, 0:T])
            p.inc(i, "vvar")
            vector.wait_ge(p.sem("sact"), 1)
            fence(vector.reciprocal(out=sd_sb[:], in_=sd_sb[:]))
            i = vector.tensor_mul(out=m_sb[:], in0=m_sb[:], in1=sd_sb[:])
            p.inc(i, "vmr")
            for u in range(NU):
                _, _, isqk = udst(u)
                if nqk(u) >= 1:
                    vector.wait_ge(p.sem("gyb"), nqk(u))
                if u >= 1:
                    _, _, pqk = udst(u - 1)
                    if pqk:
                        vector.wait_ge(p.sem("vyr"), 4 * nqk(u))
                    else:
                        vector.wait_ge(p.sem("vy2"), 4 * (u - 40))
                for tc in range(TCH):
                    vector.wait_ge(p.sem("pez"), 4 * u + tc + 1)
                    i = vector.tensor_mul(out=t1s[:, 512 * tc:512 * (tc + 1)], in0=bank(tc),
                                          in1=rb[:, 512 * tc:512 * (tc + 1)])
                    p.inc(i, "vt1")
                vector.wait_ge(p.sem("vt1"), p.cnt["vt1"])
                vector.wait_ge(p.sem("scs"), u + 1)
                for tc in range(TCH):
                    i = vector.scalar_tensor_tensor(
                        out=t1s[:, 512 * tc:512 * (tc + 1)],
                        in0=mrb[:, 512 * tc:512 * (tc + 1)], scalar=csn[u % 2][:, 0:1],
                        in1=t1s[:, 512 * tc:512 * (tc + 1)], op0=ALU.mult, op1=ALU.add)
                    p.inc(i, "vy")
                vector.wait_ge(p.sem("vy"), p.cnt["vy"])
                if not isqk:
                    if u >= 2:
                        sto.wait(vector, u - 2)
                    for tc in range(TCH):
                        i = vector.tensor_copy(out=yst[u % 2][:, 512 * tc:512 * (tc + 1)],
                                               in_=t1s[:, 512 * tc:512 * (tc + 1)])
                        p.inc(i, "vy2")
                else:
                    vector.wait_ge(p.sem("gyb"), nqk(u) + 1)
                    for tc in range(TCH):
                        i = vector.tensor_mul(out=t1s[:, 512 * tc:512 * (tc + 1)],
                                              in0=t1s[:, 512 * tc:512 * (tc + 1)],
                                              in1=cosT[:, 512 * tc:512 * (tc + 1)])
                        p.inc(i, "vu1")
                    vector.wait_ge(p.sem("vu1"), p.cnt["vu1"])
                    if u >= 2:
                        sto.wait(vector, u - 2)
                    for tc in range(TCH):
                        vector.wait_ge(p.sem("perot"), 4 * nqk(u) + tc + 1)
                        i = vector.tensor_mul(out=yst[u % 2][:, 512 * tc:512 * (tc + 1)],
                                              in0=bank(4 + tc),
                                              in1=sinT[:, 512 * tc:512 * (tc + 1)])
                        p.inc(i, "vu2")
                    vector.wait_ge(p.sem("vu2"), p.cnt["vu2"])
                    for tc in range(TCH):
                        i = vector.tensor_add(out=yst[u % 2][:, 512 * tc:512 * (tc + 1)],
                                              in0=yst[u % 2][:, 512 * tc:512 * (tc + 1)],
                                              in1=t1s[:, 512 * tc:512 * (tc + 1)])
                        p.inc(i, "vyr")

    return nc


def build_attn_core(layer, NH=32, p=None, io=None):
    if p is None:
        p = P()
        io = {"qr": p.inp("qr", [D, T], BF16), "kr": p.inp("kr", [1024, T], BF16),
              "vv": p.inp("vv", [1024, T], BF16), "ao": p.outp("ao", [D, T], BF16)}
        if layer == 1:
            io.update({"hp1": p.inp("hp1", [32, 128, 128]), "hb1": p.inp("hb1", [128, 32]),
                       "hp2": p.inp("hp2", [32, 128, 128]), "hb2": p.inp("hb2", [128, 32])})
    nc = p.nc
    pref = p.new_phase()
    ctx = ExitStack()
    sb = lambda name, shape, dt=F32: ctx.enter_context(nc.sbuf_tensor(pref + name, shape, dt))
    ps = lambda name, shape, dt=F32: ctx.enter_context(nc.psum_tensor(pref + name, shape, dt))
    const = lambda name, arr: nc.inline_tensor(np.ascontiguousarray(arr), pref + name).ap()
    lq = p.stream("lq")
    gh = p.stream("gh", 32)
    soa = p.stream("soa")
    qr, kr, vv, ao = io["qr"], io["kr"], io["vv"], io["ao"]
    L1 = (layer == 1)
    if L1:
        hp1, hb1, hp2, hb2 = io["hp1"], io["hb1"], io["hp2"], io["hb2"]
    onesP_c = const("onesP_c", np.ones((128, 1), dtype=BF))
    ident_c = const("ident_c", np.eye(128, dtype=BF))
    masks_c = const("masks_c", np_diag_masks().astype(BF))

    def WCf(g, tt):
        return 512 * g + 128 * (tt + 1)

    if L1:
        th_np = np.zeros((128, 16), np.float32)
        m4c_np = np.zeros((128, 16), np.float32)
        f4c_np = np.zeros((128, 4), np.float32)
        for g_ in range(4):
            for tt_ in range(4):
                th_np[:, 4 * g_ + tt_] = 409.5 if tt_ < 2 else 819.5 - WCf(g_, tt_)
                m4c_np[:, 4 * g_ + tt_] = KREM if tt_ < 2 else KREM - WCf(g_, tt_) / 2.0
        f4c_np[:, 0:2] = -1.0
        f4c_np[:, 2:4] = -0.5
        thc_c = const("thc_c", th_np)
        m4c_c = const("m4c_c", m4c_np)
        f4c_c = const("f4c_c", f4c_np)

    krt = sb("krt", [128, 8 * T], BF16)

    def krv(kv):
        return krt[:].rearrange("p (c t) -> p c t", c=8)[:, kv, :]

    vtok = sb("vtok", [128, 8 * 16 * 128], BF16)

    def vtv(vc, tb):
        return vtok[:].rearrange("p (c b d) -> p c b d", c=8, b=16)[:, vc, tb, :]

    vvb = sb("vvb", [128, T], BF16)
    qh_ = sb("qh", [128, 1, T], BF16); qhb = [qh_[:, 0], qh_[:, 0]]
    onesP = sb("onesP", [128, 1], BF16)
    ident = sb("ident", [128, 128], BF16)
    masks = sb("masks", [128, 4 * 512], BF16)
    zeros128 = sb("zeros128", [128, 128], BF16)
    if L1:
        thc = sb("thc", [128, 16])
        m4c = sb("m4c", [128, 16])
        f4c = sb("f4c", [128, 4])

    def maskv(tt):
        return masks[:, 512 * tt:512 * (tt + 1)]

    s_t_ = sb("s_t", [128, 4, T])
    st4 = [s_t_[:, i] for i in range(4)]
    p4_ = sb("p4", [128, 2, 4, T], BF16)

    def p4v(g, tt):
        return p4_[:, g % 2, tt]

    pT_ = sb("pT", [128, 3, 512], BF16); pT = [pT_[:, 0], pT_[:, 1], pT_[:, 2]]
    aost_ = sb("aost", [128, 2, 512], BF16); aost = [aost_[:, 0], aost_[:, 1]]
    mx4_ = sb("mx4", [128, 2, 4])
    mxs4_ = sb("mxs4", [128, 2, 4])
    lt4_ = sb("lt4", [128, 2, 4])
    lr4_ = sb("lr4", [128, 2, 4])
    if L1:
        p1b_ = sb("p1b", [128, 2, 128], BF16); p1b = [p1b_[:, 0], p1b_[:, 1]]
        p2b_ = sb("p2b", [128, 2, 128], BF16); p2b = [p2b_[:, 0], p2b_[:, 1]]
        hb1t = sb("hb1t", [128, 32])
        hb2t = sb("hb2t", [128, 32])
        u2t = sb("u2t", [128, T], BF16)
        qh_t = sb("qh_t", [128, T], BF16)
        kh_t = sb("kh_t", [128, T], BF16)
        sim4_ = sb("sim4", [128, 2, 4, T], BF16)

        def sim4v(g, tt):
            return sim4_[:, g % 2, tt]

        eq4 = sb("eq4", [128, 4 * T], BF16)

        def eq4v(tt):
            return eq4[:].rearrange("p (c t) -> p c t", c=4)[:, tt, :]

        def pref4v(g, tt):
            return p4_[:, g % 2, tt].bitcast(FP16)

        sel_ = sb("sel", [128, 2, 9 * 4])

        def selv(g, k):
            return sel_[:, g % 2, 4 * k:4 * (k + 1)]

    OB = 5 if L1 else 4
    NSL = 2 if L1 else 3
    ps_f = ps("ps_f", [128, (6 if L1 else 5) * 512])
    ps_bf = ps("ps_bf", [128, 1024 * NSL], BF16)

    def bank(b, rows=128):
        return ps_f[0:rows, 512 * b:512 * (b + 1)]

    def tslot(au):
        return ps_bf[:, 1024 * (au % NSL):1024 * (au % NSL) + 512]

    CH_PER_H = 10
    AV_PER_H = 40

    def cc_base(h, g):
        return CH_PER_H * h + (g * (g + 1)) // 2

    def av_base(h, g):
        return AV_PER_H * h + 2 * g * (g + 1)

    evA = {}
    evV = {}
    na = nv = 0
    for h in range(NH):
        for g in range(TCH):
            nch = g + 1
            for tt in range(4):
                for c in range(nch):
                    cc = 4 * cc_base(h, g) + tt * nch + c
                    if c == nch - 1:
                        nv += 1
                        evV[cc] = nv
                    else:
                        na += 1
                        evA[cc] = na
    scA = {}
    scV = {}
    na = nv = 0
    for h in range(NH):
        for g in range(TCH):
            nch = g + 1
            for tt in range(4):
                for c in range(nch):
                    cc = 4 * cc_base(h, g) + tt * nch + c
                    diag = (c == nch - 1)
                    if diag or ((not L1) and cc % 2 == 1):
                        nv += 1
                        scV[cc] = nv
                    else:
                        na += 1
                        scA[cc] = na

    def ev_done(eng_w, cc):
        if cc in scA:
            eng_w.wait_ge(p.sem("evca"), scA[cc])
        else:
            eng_w.wait_ge(p.sem("evcv"), scV[cc])

    def evs_done(eng_w, cc):
        if cc in evA:
            eng_w.wait_ge(p.sem("evsa"), evA[cc])
        else:
            eng_w.wait_ge(p.sem("evsv"), evV[cc])

    def grp_last(m, base, n):
        return max((m[base + x] for x in range(n) if base + x in m), default=0)

    with ctx, nc.Block() as blk:
        @blk.sync
        def _(sync):
            sync = p.wrap(sync)
            p.inc(sync.dma_start(out=onesP[:], in_=onesP_c), "ld1", 16)
            p.inc(sync.dma_start(out=ident[:], in_=ident_c), "ld1", 16)
            i = sync.dma_start(out=masks[:].rearrange("p (c n) -> p c n", c=4),
                               in_=masks_c.rearrange("c p n -> p c n"))
            p.inc(i, "ld1", 16)
            if L1:
                p.inc(sync.dma_start(out=hb1t[:], in_=hb1), "ld1", 16)
                p.inc(sync.dma_start(out=hb2t[:], in_=hb2), "ld1", 16)
                p.inc(sync.dma_start(out=thc[:], in_=thc_c), "ld1", 16)
                p.inc(sync.dma_start(out=m4c[:], in_=m4c_c), "ld1", 16)
                p.inc(sync.dma_start(out=f4c[:], in_=f4c_c), "ld1", 16)
            for c in range(8):
                p.inc(sync.dma_start(out=krv(c), in_=kr[128 * c:128 * (c + 1), :]), "ldk", 16)
            for vc in range(8):
                if vc >= 1:
                    sync.wait_ge(p.sem("pev"), 4 * vc)
                p.inc(sync.dma_start(out=vvb[:], in_=vv[128 * vc:128 * (vc + 1), :]), "ldv", 16)
            for h in range(NH):
                if h >= 1:
                    sync.wait_ge(p.sem("hdone"), h)
                i = sync.dma_start(out=qhb[h % 2], in_=qr[128 * h:128 * (h + 1), :])
                lq.inc(i, h)

        if L1:
            @blk.gpsimd
            def _(gp):
                gp = p.wrap(gp)
                for h in range(NH):
                    if h >= 2:
                        gp.wait_ge(p.sem("peh2"), 8 * (h - 1))
                    i = gp.dma_start(out=p1b[h % 2], in_=hp1[h])
                    gh.inc(i, h, 16)
                    i = gp.dma_start(out=p2b[h % 2], in_=hp2[h])
                    gh.inc(i, h, 16)

        @blk.tensor
        def _(tensor):
            tensor = p.wrap(tensor)
            tensor.wait_ge(p.sem("ld1"), 16 * (8 if L1 else 3))
            tensor.wait_ge(p.sem("vz"), 1)

            for vc in range(8):
                tensor.wait_ge(p.sem("ldv"), 16 * (vc + 1))
                for q in range(4):
                    n = 4 * vc + q
                    if n >= 2:
                        tensor.wait_ge(p.sem("sev"), n - 1)
                    for kk in range(4):
                        tb = 4 * q + kk
                        i = tensor.transpose(ps_bf[:, 1024 * (n % 2) + 128 * kk:
                                                   1024 * (n % 2) + 128 * (kk + 1)],
                                             vvb[:, 128 * tb:128 * (tb + 1)], ident[:])
                    p.inc(i, "pev")
            tensor.wait_ge(p.sem("sev"), 32)
            tensor.wait_ge(p.sem("ldk"), 16 * 8)

            def pe_sims(h, g):
                kv = h // 4
                nch = g + 1
                base = 4 * cc_base(h, g)
                tensor.wait_ge(p.sem("vsgn"), 2 * h + 2)
                for tt in range(4):
                    Tt = 4 * g + tt
                    for c in range(nch):
                        cc = base + tt * nch + c
                        if cc - 4 >= base:
                            evs_done(tensor, cc - 4)
                        elif cc >= 4:
                            ev_done(tensor, cc - 4)
                        wd = 128 * (tt + 1) if c == g else 512
                        i = tensor.matmul(bank(cc % 4)[:, 0:wd],
                                          qh_t[:, 128 * Tt:128 * (Tt + 1)],
                                          kh_t[:, 512 * c:512 * c + wd], start=True, stop=True)
                        p.inc(i, "pesim")

            def pe_scores(h, g, part):
                kv = h // 4
                nch = g + 1
                base = 4 * cc_base(h, g)
                if L1 and part == 0:
                    la = grp_last(evA, base, 4 * nch)
                    lv = grp_last(evV, base, 4 * nch)
                    if la:
                        tensor.wait_ge(p.sem("evsa"), la)
                    if lv:
                        tensor.wait_ge(p.sem("evsv"), lv)
                for tt in range(4):
                    Tt = 4 * g + tt
                    for c in range(nch):
                        cc = base + tt * nch + c
                        idx = cc - base
                        if (part == 0) != (idx < 4):
                            continue
                        if cc >= 4:
                            ev_done(tensor, cc - 4)
                        wd = 128 * (tt + 1) if c == g else 512
                        i = tensor.matmul(bank(cc % 4)[:, 0:wd],
                                          qhb[h % 2][:, 128 * Tt:128 * (Tt + 1)],
                                          krv(kv)[:, 512 * c:512 * c + wd], start=True, stop=True)
                        p.inc(i, "pesc")

            def pe_av(h, g):
                kv = h // 4
                nch = g + 1
                tensor.wait_ge(p.sem("vnp"), 16 * h + 4 * g + 4)
                if 4 * h + g >= 1:
                    tensor.wait_ge(p.sem("vao"), 4 * h + g)
                nau = 4 * nch
                ab = av_base(h, g)

                def emit_T(u):
                    au = ab + u
                    if au >= NSL:
                        tensor.wait_ge(p.sem("sevp"), au - NSL + 1)
                    c, kk = divmod(u, 4)
                    for tt in range(4):
                        src = (zeros128[:] if (c == g and tt < kk) else
                               p4v(g, tt)[:, 512 * c + 128 * kk:512 * c + 128 * (kk + 1)])
                        i = tensor.transpose(tslot(au)[:, 128 * tt:128 * (tt + 1)],
                                             src, ident[:])
                    p.inc(i, "pet")

                for w in range(min(NSL - 1, nau)):
                    emit_T(w)
                for u in range(nau):
                    if u + NSL - 1 < nau:
                        emit_T(u + NSL - 1)
                    au = ab + u
                    tensor.wait_ge(p.sem("sevp"), au + 1)
                    c, kk = divmod(u, 4)
                    i = tensor.matmul(bank(OB), vtv(kv, 4 * c + kk), pT[au % NSL],
                                      start=(u == 0), stop=(u == nau - 1))
                    p.inc(i, "peav")

            for h in range(NH):
                kv = h // 4
                lq.wait(tensor, h)
                if L1:
                    gh.wait(tensor, h)
                    if h >= 1:
                        tensor.wait_ge(p.sem("vao"), 4 * h)
                    for which in range(2):
                        src = qhb[h % 2] if which == 0 else krv(kv)
                        w2 = 2 * h + which
                        tensor.wait_ge(p.sem("shs2"), 4 * w2)
                        for tc in range(TCH):
                            if 8 * h + 4 * which + tc >= 2:
                                tensor.wait_ge(p.sem("shs"), 4 * w2 + tc - 1)
                            i = tensor.matmul(bank(4 + tc % 2), p1b[h % 2],
                                              src[:, 512 * tc:512 * (tc + 1)], start=True, stop=True)
                            p.inc(i, "peh1")
                        tensor.wait_ge(p.sem("vu2h"), w2 + 1)
                        for tc in range(TCH):
                            if tc >= 2:
                                tensor.wait_ge(p.sem("shs2"), 4 * w2 + tc - 1)
                            i = tensor.matmul(bank(4 + tc % 2), p2b[h % 2],
                                              u2t[:, 512 * tc:512 * (tc + 1)], start=True, stop=True)
                            p.inc(i, "peh2")
                for g in range(TCH):
                    if L1:
                        pe_sims(h, g)
                    pe_scores(h, g, 0)
                    if g >= 1:
                        pe_av(h, g - 1)
                    pe_scores(h, g, 1)
                pe_av(h, TCH - 1)
                tensor.nop().then_inc(p.sem("hdone"), 1)
                p.cnt["hdone"] += 1

        @blk.scalar
        def _(scalar):
            scalar = p.wrap(scalar)
            for vc in range(8):
                for q in range(4):
                    n = 4 * vc + q
                    scalar.wait_ge(p.sem("pev"), n + 1)
                    i = scalar.copy(out=vtok[:].rearrange("p (c b d) -> p c b d", c=8, b=16)
                                    [:, vc, 4 * q:4 * (q + 1), :],
                                    in_=ps_bf[:, 1024 * (n % 2):1024 * (n % 2) + 512])
                    p.inc(i, "sev")

            def act_stage_a(h, g):
                nch = g + 1
                base = 4 * cc_base(h, g)
                if 4 * h + g >= 2:
                    scalar.wait_ge(p.sem("vkp"), 16 * h + 4 * (g - 1))
                for tt in range(4):
                    for c in range(nch):
                        cc = base + tt * nch + c
                        if cc not in evA:
                            continue
                        scalar.wait_ge(p.sem("pesim"), cc + 1)
                        i = scalar.copy(out=sim4v(g, tt)[:, 512 * c:512 * (c + 1)],
                                        in_=bank(cc % 4))
                        p.inc(i, "evsa")
                ITs = 8 * (4 * h + g)
                lvV_gs = grp_last(evV, base, 4 * nch)
                if lvV_gs:
                    scalar.wait_ge(p.sem("evsv"), lvV_gs)
                for j in range(8):
                    scalar.wait_ge(p.sem("bnc"), ITs + j + 1)
                    for tt in (2, 3):
                        Ws = WCf(g, tt)
                        dst = selv(g, 3) if j < 7 else selv(g, 5)
                        i = scalar.activation(out=eq4v(tt)[:, 0:Ws],
                                              in_=sim4v(g, tt)[:, 0:Ws], func=AF.Sign,
                                              bias=selv(g, 2)[:, tt:tt + 1],
                                              accum_out=dst[:, tt:tt + 1])
                        p.inc(i, "bsc")

            def act_b1(h, g):
                nch = g + 1
                base = 4 * cc_base(h, g)
                for tt in range(4):
                    tile = 16 * h + 4 * g + tt
                    if tile >= 4:
                        scalar.wait_ge(p.sem("sexp"), tile - 3)
                    for c in range(nch):
                        cc = base + tt * nch + c
                        if cc not in scA:
                            continue
                        scalar.wait_ge(p.sem("pesc"), cc + 1)
                        i = scalar.copy(out=st4[tt][:, 512 * c:512 * (c + 1)],
                                        in_=bank(cc % 4))
                        p.inc(i, "evca")
                    scalar.wait_ge(p.sem("vmxs"), tile + 1)
                    if 4 * h + g >= 2:
                        scalar.wait_ge(p.sem("pet"), av_base(h, g - 1))
                    Wc = WCf(g, tt)
                    if L1:
                        i = scalar.activation(out=p4v(g, tt)[:, 0:Wc],
                                              in_=st4[tt][:, 0:Wc],
                                              func=AF.Exp, scale=SCALE,
                                              bias=mxs4_[:, g % 2, tt:tt + 1])
                    else:
                        i = scalar.activation(out=p4v(g, tt)[:, 0:Wc],
                                              in_=st4[tt][:, 0:Wc],
                                              func=AF.Exp, scale=SCALE,
                                              bias=mxs4_[:, g % 2, tt:tt + 1],
                                              accum_out=lt4_[:, g % 2, tt:tt + 1])
                    p.inc(i, "sexp")

            def act_b2(h, g):
                nch = g + 1
                if L1:
                    for u in range(4 * nch):
                        au = av_base(h, g) + u
                        scalar.wait_ge(p.sem("pet"), au + 1)
                        if au >= NSL:
                            scalar.wait_ge(p.sem("peav"), au - NSL + 1)
                        i = scalar.copy(out=pT[au % NSL], in_=tslot(au))
                        p.inc(i, "sevp")
                hg = 4 * h + g
                scalar.wait_ge(p.sem("peav"), av_base(h, g) + 4 * nch)
                if hg >= 2:
                    soa.wait(scalar, hg - 2)
                i = scalar.copy(out=aost[hg % 2], in_=bank(OB))
                p.inc(i, "vao")
                i = scalar.dma_start(out=ao[128 * h:128 * (h + 1), 512 * g:512 * (g + 1)],
                                     in_=aost[hg % 2])
                soa.inc(i, hg, 16)

            for h in range(NH):
                if L1:
                    for which in range(2):
                        w2 = 2 * h + which
                        scalar.wait_ge(p.sem("shs2"), 4 * w2)
                        for tc in range(TCH):
                            scalar.wait_ge(p.sem("peh1"), 4 * w2 + tc + 1)
                            i = scalar.activation(out=u2t[:, 512 * tc:512 * (tc + 1)],
                                                  in_=bank(4 + tc % 2), func=AF.Silu,
                                                  bias=hb1t[:, h:h + 1])
                            p.inc(i, "shs")
                for g in range(TCH):
                    if g >= 1:
                        act_b1(h, g - 1)
                    if L1:
                        act_stage_a(h, g)
                    if g >= 1:
                        act_b2(h, g - 1)
                act_b1(h, TCH - 1)
                act_b2(h, TCH - 1)
            soa.wait_all(scalar, 4 * NH)

        @blk.vector
        def _(vector):
            vector = p.wrap(vector)
            i = vector.memset(zeros128[:], 0.0)
            p.inc(i, "vz")
            vector.wait_ge(p.sem("ld1"), 16 * (8 if L1 else 3))

            def fence(i):
                p.inc(i, "vr")
                vector.wait_ge(p.sem("vr"), p.cnt["vr"])


            def dve_stage_a(h, g):
                nch = g + 1
                W = 512 * nch
                base = 4 * cc_base(h, g)
                if 4 * h + g >= 2:
                    vector.wait_ge(p.sem("vkp"), 16 * h + 4 * (g - 1))
                for tt in range(4):
                    cc = base + tt * nch + (nch - 1)
                    dw = 128 * (tt + 1)
                    vector.wait_ge(p.sem("pesim"), cc + 1)
                    vector.wait_ge(p.sem("evsv"), evV[cc] - 1)
                    i = vector.scalar_tensor_tensor(
                        out=sim4v(g, tt)[:, 512 * g:512 * g + dw], in0=bank(cc % 4)[:, 0:dw],
                        scalar=1.0, in1=maskv(tt)[:, 0:dw], op0=ALU.mult, op1=ALU.add)
                    p.inc(i, "evsv")
                base4, cand4, negc4 = selv(g, 0), selv(g, 1), selv(g, 2)
                cnt4, cond4, gtc4 = selv(g, 3), selv(g, 4), selv(g, 5)
                m4, t4, t14 = selv(g, 6), selv(g, 7), selv(g, 8)
                IT = 8 * (4 * h + g)
                la = grp_last(evA, base, 4 * nch)
                if la:
                    vector.wait_ge(p.sem("evsa"), la)
                fence(vector.memset(base4[:], -129.0))
                fence(vector.tensor_scalar(out=cand4[:], in0=base4[:], scalar1=128.0,
                                           scalar2=None, op0=ALU.add))
                for k in range(6, -1, -1):
                    j = 6 - k
                    i = vector.tensor_scalar(out=negc4[:], in0=cand4[:], scalar1=-1.0,
                                             scalar2=-0.5, op0=ALU.mult, op1=ALU.add)
                    p.inc(i, "bnc")
                    for tt in range(2):
                        i = vector.tensor_scalar(
                            out=eq4v(tt)[:, 0:WCf(g, tt)], in0=sim4v(g, tt)[:, 0:WCf(g, tt)],
                            scalar1=cand4[:, tt:tt + 1], scalar2=0.0,
                            op0=ALU.is_gt, op1=ALU.add, accum_out=cnt4[:, tt:tt + 1])
                        p.inc(i, "vr")
                    vector.wait_ge(p.sem("vr"), p.cnt["vr"])
                    vector.wait_ge(p.sem("bsc"), 2 * (IT + j + 1))
                    fence(vector.tensor_tensor(out=cond4[:], in0=cnt4[:],
                                               in1=thc[:, 4 * g:4 * g + 4], op=ALU.is_gt))
                    fence(vector.scalar_tensor_tensor(out=base4[:], in0=cond4[:],
                                                      scalar=float(2 ** (k + 1)), in1=base4[:],
                                                      op0=ALU.mult, op1=ALU.add))
                    if k > 0:
                        fence(vector.tensor_scalar(out=cand4[:], in0=base4[:],
                                                   scalar1=float(2 ** k),
                                                   scalar2=None, op0=ALU.add))
                fence(vector.tensor_scalar(out=t4[:], in0=base4[:], scalar1=1.0,
                                           scalar2=None, op0=ALU.add))
                i = vector.tensor_scalar(out=negc4[:], in0=t4[:], scalar1=-1.0,
                                         scalar2=-0.5, op0=ALU.mult, op1=ALU.add)
                p.inc(i, "bnc")
                for tt in range(2):
                    i = vector.tensor_scalar(
                        out=eq4v(tt)[:, 0:WCf(g, tt)], in0=sim4v(g, tt)[:, 0:WCf(g, tt)],
                        scalar1=t4[:, tt:tt + 1], scalar2=0.0,
                        op0=ALU.is_gt, op1=ALU.add, accum_out=gtc4[:, tt:tt + 1])
                    p.inc(i, "vr")
                vector.wait_ge(p.sem("vr"), p.cnt["vr"])
                vector.wait_ge(p.sem("bsc"), 2 * (IT + 8))
                fence(vector.tensor_tensor(out=m4[:], in0=gtc4[:], in1=f4c[:], op=ALU.mult))
                fence(vector.tensor_tensor(out=m4[:], in0=m4[:],
                                           in1=m4c[:, 4 * g:4 * g + 4], op=ALU.add))
                fence(vector.tensor_scalar(out=t14[:], in0=t4[:], scalar1=1.0,
                                           scalar2=None, op0=ALU.add))
                for tt in range(4):
                    Wc = WCf(g, tt)
                    i = vector.tensor_scalar(out=eq4v(tt)[:, 0:Wc], in0=sim4v(g, tt)[:, 0:Wc],
                                             scalar1=t4[:, tt:tt + 1], scalar2=None,
                                             op0=ALU.is_equal)
                    p.inc(i, "vr")
                vector.wait_ge(p.sem("vr"), p.cnt["vr"])
                if 4 * h + g >= 2:
                    vector.wait_ge(p.sem("pet"), av_base(h, g - 1))
                for tt in range(4):
                    Wc = WCf(g, tt)
                    i = vector.tensor_tensor_scan(out=pref4v(g, tt)[:, 0:Wc],
                                                  data0=eq4v(tt)[:, 0:Wc],
                                                  data1=eq4v(tt)[:, 0:Wc], initial=0.0,
                                                  op0=ALU.add, op1=ALU.bypass)
                    p.inc(i, "vr")
                vector.wait_ge(p.sem("vr"), p.cnt["vr"])
                for tt in range(4):
                    Wc = WCf(g, tt)
                    i = vector.scalar_tensor_tensor(out=eq4v(tt)[:, 0:Wc],
                                                    in0=pref4v(g, tt)[:, 0:Wc],
                                                    scalar=m4[:, tt:tt + 1],
                                                    in1=sim4v(g, tt)[:, 0:Wc],
                                                    op0=ALU.is_le, op1=ALU.add)
                    p.inc(i, "vr")
                vector.wait_ge(p.sem("vr"), p.cnt["vr"])
                for tt in range(4):
                    Wc = WCf(g, tt)
                    i = vector.tensor_scalar(out=sim4v(g, tt)[:, 0:Wc], in0=eq4v(tt)[:, 0:Wc],
                                             scalar1=t14[:, tt:tt + 1], scalar2=None,
                                             op0=ALU.is_ge)
                    p.inc(i, "vr")
                vector.wait_ge(p.sem("vr"), p.cnt["vr"])

            def dve_stage_b(h, g):
                nch = g + 1
                W = 512 * nch
                base = 4 * cc_base(h, g)
                for tt in range(4):
                    tile = 16 * h + 4 * g + tt
                    if tile >= 4:
                        vector.wait_ge(p.sem("sexp"), tile - 3)
                    for c in range(nch):
                        cc = base + tt * nch + c
                        if cc not in scV:
                            continue
                        diag = (c == nch - 1)
                        vector.wait_ge(p.sem("pesc"), cc + 1)
                        if diag:
                            dw = 128 * (tt + 1)
                            i = vector.scalar_tensor_tensor(
                                out=st4[tt][:, 512 * c:512 * c + dw], in0=bank(cc % 4)[:, 0:dw],
                                scalar=1.0, in1=maskv(tt)[:, 0:dw], op0=ALU.mult, op1=ALU.add)
                        else:
                            i = vector.tensor_copy(out=st4[tt][:, 512 * c:512 * (c + 1)],
                                                   in_=bank(cc % 4))
                        p.inc(i, "evcv")
                    la = grp_last(scA, base + tt * nch, nch)
                    lv = grp_last(scV, base + tt * nch, nch)
                    if la:
                        vector.wait_ge(p.sem("evca"), la)
                    if lv:
                        vector.wait_ge(p.sem("evcv"), lv)
                    i = vector.tensor_reduce(out=mx4_[:, g % 2, tt:tt + 1],
                                             in_=st4[tt][:, 0:WCf(g, tt)],
                                             axis=AX.X, op=ALU.max, negate=True)
                    fence(i)
                    i = vector.tensor_scalar(out=mxs4_[:, g % 2, tt:tt + 1],
                                             in0=mx4_[:, g % 2, tt:tt + 1],
                                             scalar1=SCALE, scalar2=None, op0=ALU.mult)
                    p.inc(i, "vmxs")
                for tt in range(4):
                    tile = 16 * h + 4 * g + tt
                    if L1:
                        Wc = WCf(g, tt)
                        vector.wait_ge(p.sem("sexp"), tile + 1)
                        i = vector.scalar_tensor_tensor(
                            out=p4v(g, tt)[:, 0:Wc], in0=p4v(g, tt)[:, 0:Wc], scalar=1.0,
                            in1=sim4v(g, tt)[:, 0:Wc], op0=ALU.mult, op1=ALU.mult,
                            accum_out=lt4_[:, g % 2, tt:tt + 1])
                        p.inc(i, "vkp")
                if L1:
                    vector.wait_ge(p.sem("vkp"), 16 * h + 4 * g + 4)
                else:
                    vector.wait_ge(p.sem("sexp"), 16 * h + 4 * g + 4)
                i = vector.reciprocal(out=lr4_[:, g % 2, :], in_=lt4_[:, g % 2, :])
                p.inc(i, "vr")
                vector.wait_ge(p.sem("vr"), p.cnt["vr"])
                for tt in range(4):
                    Wc = WCf(g, tt)
                    i = vector.tensor_scalar(out=p4v(g, tt)[:, 0:Wc], in0=p4v(g, tt)[:, 0:Wc],
                                             scalar1=lr4_[:, g % 2, tt:tt + 1], scalar2=None,
                                             op0=ALU.mult)
                    p.inc(i, "vnp")

            for h in range(NH):
                kv = h // 4
                if L1:
                    for which in range(2):
                        src = qhb[h % 2] if which == 0 else krv(kv)
                        dst = qh_t if which == 0 else kh_t
                        w2 = 2 * h + which
                        vector.wait_ge(p.sem("shs"), 4 * w2 + 4)
                        i = vector.tensor_add(out=u2t[:], in0=u2t[:], in1=src)
                        p.inc(i, "vu2h")
                        vector.wait_ge(p.sem("vu2h"), w2 + 1)
                        if h >= 1:
                            vector.wait_ge(p.sem("sexp"), 16 * h)
                        for tc in range(TCH):
                            vector.wait_ge(p.sem("peh2"), 4 * w2 + tc + 1)
                            i = vector.scalar_tensor_tensor(
                                out=st4[0][:, 512 * tc:512 * (tc + 1)], in0=bank(4 + tc % 2),
                                scalar=hb2t[:, h:h + 1], in1=u2t[:, 512 * tc:512 * (tc + 1)],
                                op0=ALU.add, op1=ALU.add)
                            p.inc(i, "shs2")
                        vector.wait_ge(p.sem("shs2"), 4 * w2 + 4)
                        i = vector.tensor_scalar(out=dst[:], in0=st4[0][:], scalar1=0.0,
                                                 scalar2=2.0, op0=ALU.is_ge, op1=ALU.mult)
                        fence(i)
                        i = vector.tensor_scalar_sub(dst[:], dst[:], 1.0)
                        p.inc(i, "vsgn")
                def dve_pt_evacs(g):
                    nch = g + 1
                    for u in range(4 * nch):
                        au = av_base(h, g) + u
                        vector.wait_ge(p.sem("pet"), au + 1)
                        if au >= NSL:
                            vector.wait_ge(p.sem("peav"), au - NSL + 1)
                        i = vector.tensor_copy(out=pT[au % NSL], in_=tslot(au))
                        p.inc(i, "sevp")

                for g in range(TCH):
                    if g >= 1:
                        dve_stage_b(h, g - 1)
                        if not L1:
                            dve_pt_evacs(g - 1)
                    if L1:
                        dve_stage_a(h, g)
                dve_stage_b(h, TCH - 1)
                if not L1:
                    dve_pt_evacs(TCH - 1)

    return nc



def build_attn_out(p=None, io=None):
    if p is None:
        p = P()
        io = {"x": p.inp("x", [D, T]), "aoin": p.inp("aoin", [D, T], BF16),
              "Wo": p.inp("Wo", [D, D]), "xo": p.outp("xo", [D, T])}
    nc = p.nc
    pref = p.new_phase()
    ctx = ExitStack()
    sb = lambda name, shape, dt=F32: ctx.enter_context(nc.sbuf_tensor(pref + name, shape, dt))
    ps = lambda name, shape, dt=F32: ctx.enter_context(nc.psum_tensor(pref + name, shape, dt))
    const = lambda name, arr: nc.inline_tensor(np.ascontiguousarray(arr), pref + name).ap()
    ga = p.stream("ga")
    gw = p.stream("gw")
    ldx = p.stream("ldx")
    sto = p.stream("sto")
    x, aoin, Wo, xo = io["x"], io["aoin"], io["Wo"], io["xo"]

    ab = sb("ab", [128, NCH * T], BF16)

    def abv(ic):
        return ab[:].rearrange("p (c t) -> p c t", c=NCH)[:, ic, :]

    wbuf_ = sb("wbuf", [128, 2, NCH * 128], BF16); wbuf = [wbuf_[:, 0], wbuf_[:, 1]]

    def wbv(b, ic):
        return wbuf[b].rearrange("p (c o) -> p c o", c=NCH)[:, ic, :]

    xres_ = sb("xres", [128, 2, 512]); xres = [xres_[:, 0], xres_[:, 1]]
    outsb_ = sb("outsb", [128, 2, 512]); outsb = [outsb_[:, 0], outsb_[:, 1]]
    ps_all = ps("ps_all", [128, 8 * 512])

    def bank(b):
        return ps_all[:, 512 * b:512 * (b + 1)]

    with ctx, nc.Block() as blk:
        @blk.sync
        def _(sync):
            sync = p.wrap(sync)
            for ic in range(NCH):
                i = sync.dma_start(out=abv(ic), in_=aoin[128 * ic:128 * (ic + 1), :])
                ga.inc(i, ic)
            for oc in range(NCH):
                for tc in range(TCH):
                    g = TCH * oc + tc
                    if g >= 2:
                        sync.wait_ge(p.sem("vy"), g - 1)
                    i = sync.dma_start(out=xres[g % 2],
                                       in_=x[128 * oc:128 * (oc + 1), 512 * tc:512 * (tc + 1)])
                    ldx.inc(i, g)

        @blk.gpsimd
        def _(gp):
            gp = p.wrap(gp)
            for oc in range(NCH):
                if oc >= 2:
                    gp.wait_ge(p.sem("pez"), TCH * (oc - 1))
                i = gp.dma_start(out=wbuf[oc % 2].rearrange("p (c o) -> p c o", c=NCH),
                                 in_=Wo[:, 128 * oc:128 * (oc + 1)].rearrange("(c p) o -> p c o", p=128))
                gw.inc(i, oc)

        @blk.tensor
        def _(tensor):
            tensor = p.wrap(tensor)
            ga.wait_all(tensor, NCH)
            for oc in range(NCH):
                gw.wait(tensor, oc)
                for tc in range(TCH):
                    g = TCH * oc + tc
                    if g >= 2:
                        tensor.wait_ge(p.sem("vy"), g - 1)
                    for ic in range(NCH):
                        i = tensor.matmul(bank(g % 2), wbv(oc % 2, ic),
                                          abv(ic)[:, 512 * tc:512 * (tc + 1)],
                                          start=(ic == 0), stop=(ic == NCH - 1))
                    p.inc(i, "pez")

        @blk.vector
        def _(vector):
            vector = p.wrap(vector)
            for oc in range(NCH):
                for tc in range(TCH):
                    g = TCH * oc + tc
                    vector.wait_ge(p.sem("pez"), g + 1)
                    ldx.wait(vector, g)
                    if g >= 2:
                        sto.wait(vector, g - 2)
                    i = vector.scalar_tensor_tensor(
                        out=outsb[g % 2], in0=bank(g % 2), scalar=1.0,
                        in1=xres[g % 2], op0=ALU.mult, op1=ALU.add)
                    p.inc(i, "vy")

        @blk.scalar
        def _(scalar):
            scalar = p.wrap(scalar)
            for oc in range(NCH):
                for tc in range(TCH):
                    g = TCH * oc + tc
                    scalar.wait_ge(p.sem("vy"), g + 1)
                    i = scalar.dma_start(out=xo[128 * oc:128 * (oc + 1), 512 * tc:512 * (tc + 1)],
                                         in_=outsb[g % 2])
                    sto.inc(i, g)
            sto.wait_all(scalar, NCH * TCH)

    return nc


def emit_copyout(p, src, dst, cast=False):
    nc = p.nc
    p.new_phase()
    with nc.Block() as blk:
        @blk.sync
        def _(sync):
            sync = p.wrap(sync)
            n = 0
            for ic in range(src.shape[0] // 128):
                if cast:
                    pass
                i = sync.dma_start(out=dst[128 * ic:128 * (ic + 1), :src.shape[1]],
                                   in_=src[128 * ic:128 * (ic + 1), :])
                p.inc(i, "cpy", 16)
                n += 16
            sync.wait_ge(p.sem("cpy"), p.cnt["cpy"])


def build_fused(upto=99, pad=0):
    p = P()
    nc = p.nc
    f = {"x": p.inp("x", [D, T]),
         "qw": p.inp("qw", [D, 2048], I32),
         "sc": p.inp("sc", [D, 32]), "zr": p.inp("zr", [D, 32]),
         "Wq0": p.inp("Wq0", [D, D]), "Wk0": p.inp("Wk0", [D, 1024]),
         "Wv0": p.inp("Wv0", [D, 1024]), "Wo0": p.inp("Wo0", [D, D]),
         "Wq1": p.inp("Wq1", [D, D]), "Wk1": p.inp("Wk1", [D, 1024]),
         "Wv1": p.inp("Wv1", [D, 1024]), "Wo1": p.inp("Wo1", [D, D]),
         "hp1": p.inp("hp1", [32, 128, 128]), "hb1": p.inp("hb1", [128, 32]),
         "hp2": p.inp("hp2", [32, 128, 128]), "hb2": p.inp("hb2", [128, 32]),
         "V": p.inp("V", [D, 1024]), "UT": p.inp("UT", [1024, D]),
         "St": p.inp("St", [128, 8]),
         "out": p.outp("out", [D, T])}
    dram = lambda name, shape, dt=F32: nc.dram_tensor(name, shape, dt, kind="Internal").ap()
    xA = dram("xA", [D, T])
    xB = dram("xB", [D, T])
    wT0 = dram("wT0", [NCH, D, 128], BF16)
    qrS = dram("qrS", [D, T], BF16)
    krS = dram("krS", [1024, T], BF16)
    vvS = dram("vvS", [1024, T], BF16)
    aoS = dram("aoS", [D, T], BF16)

    for c in range(8):
        build_dequant(p=p, io={"qw": f["qw"][512 * c:512 * (c + 1), :],
                               "sc": f["sc"][512 * c:512 * (c + 1), :],
                               "zr": f["zr"][512 * c:512 * (c + 1), :],
                               "wT": wT0[4 * c:4 * (c + 1)]})
    steps = [
        (1, lambda: build_ffn0(0.5, p=p, io={"x": f["x"], "w4": wT0, "xo": xA}), xA),
        (2, lambda: build_attn_proj(p=p, io={"x": xA, "Wq": f["Wq0"], "Wk": f["Wk0"],
                                             "Wv": f["Wv0"], "qr": qrS, "kr": krS,
                                             "vv": vvS}), qrS),
        (3, lambda: build_attn_core(0, p=p, io={"qr": qrS, "kr": krS, "vv": vvS,
                                                "ao": aoS}), aoS),
        (4, lambda: build_attn_out(p=p, io={"x": xA, "aoin": aoS, "Wo": f["Wo0"],
                                            "xo": xB}), xB),
        (5, lambda: build_ffn0(0.5, p=p, io={"x": xB, "w4": wT0, "xo": xA}), xA),
        (6, lambda: build_ffn1(0.5, p=p, io={"x": xA, "V": f["V"], "UT": f["UT"],
                                             "St": f["St"], "xo": xB}), xB),
        (7, lambda: build_attn_proj(p=p, io={"x": xB, "Wq": f["Wq1"], "Wk": f["Wk1"],
                                             "Wv": f["Wv1"], "qr": qrS, "kr": krS,
                                             "vv": vvS}), qrS),
        (8, lambda: build_attn_core(1, p=p, io={"qr": qrS, "kr": krS, "vv": vvS,
                                                "ao": aoS, "hp1": f["hp1"],
                                                "hb1": f["hb1"], "hp2": f["hp2"],
                                                "hb2": f["hb2"]}), aoS),
        (9, lambda: build_attn_out(p=p, io={"x": xB, "aoin": aoS, "Wo": f["Wo1"],
                                            "xo": xA}), xA),
        (10, lambda: build_ffn1(0.5, p=p, io={"x": xA, "V": f["V"], "UT": f["UT"],
                                              "St": f["St"], "xo": f["out"]}), None),
    ]
    last = None
    for k, fn, chk in steps:
        if k > upto:
            break
        fn()
        last = chk
    for _ in range(pad):
        build_attn_out(p=p, io={"x": f["x"], "aoin": qrS, "Wo": f["Wq0"], "xo": xB})
    if last is not None:
        emit_copyout(p, last.bitcast(F32) if last.dtype != F32 else last, f["out"].bitcast(last.dtype).bitcast(F32) if False else f["out"])
    return nc


_BUILT = {}


def kernel(L0_attn_x, L0_attn_Wq, L0_attn_Wk, L0_attn_Wv, L0_attn_Wo,
           L0_ffn_qweight, L0_ffn_scale, L0_ffn_zero,
           L1_attn_Wq, L1_attn_Wk, L1_attn_Wv, L1_attn_Wo,
           L1_attn_hash_proj1, L1_attn_hash_bias1, L1_attn_hash_proj2, L1_attn_hash_bias2,
           L1_ffn_U, L1_ffn_S, L1_ffn_V):
    NC = 8
    f32 = np.float32
    x0 = np.asarray(L0_attn_x, f32)
    if "fused" not in _BUILT:
        _BUILT["fused"] = build_fused()
    nc = _BUILT["fused"]
    w = {
        "qw": np.ascontiguousarray(np.asarray(L0_ffn_qweight, np.int32)),
        "sc": np.ascontiguousarray(np.asarray(L0_ffn_scale, f32)),
        "zr": np.ascontiguousarray(np.asarray(L0_ffn_zero, f32)),
        "Wq0": np.ascontiguousarray(np.asarray(L0_attn_Wq, f32)),
        "Wk0": np.ascontiguousarray(np.asarray(L0_attn_Wk, f32)),
        "Wv0": np.ascontiguousarray(np.asarray(L0_attn_Wv, f32)),
        "Wo0": np.ascontiguousarray(np.asarray(L0_attn_Wo, f32)),
        "Wq1": np.ascontiguousarray(np.asarray(L1_attn_Wq, f32).T),
        "Wk1": np.ascontiguousarray(np.asarray(L1_attn_Wk, f32).T),
        "Wv1": np.ascontiguousarray(np.asarray(L1_attn_Wv, f32).T),
        "Wo1": np.ascontiguousarray(np.asarray(L1_attn_Wo, f32).T),
        "hp1": np.ascontiguousarray(np.asarray(L1_attn_hash_proj1, f32)),
        "hb1": np.ascontiguousarray(np.asarray(L1_attn_hash_bias1, f32).reshape(32, 128).T),
        "hp2": np.ascontiguousarray(np.asarray(L1_attn_hash_proj2, f32)),
        "hb2": np.ascontiguousarray(np.asarray(L1_attn_hash_bias2, f32).reshape(32, 128).T),
        "V": np.ascontiguousarray(np.asarray(L1_ffn_V, f32)),
        "UT": np.ascontiguousarray(np.asarray(L1_ffn_U, f32).T),
        "St": np.ascontiguousarray(np.asarray(L1_ffn_S, f32).reshape(8, 128).T),
    }
    in_maps = [dict(w, x=np.ascontiguousarray(x0[c].T)) for c in range(NC)]
    res = run_bass_kernel_spmd(nc, in_maps, core_ids=list(range(NC))).results
    out = np.stack([np.asarray(res[c]["out"], f32).T for c in range(NC)], axis=0)
    return out
